# version 12
# baseline (speedup 1.0000x reference)
import contextlib
import numpy as np
import ml_dtypes
import concourse.bass as bass
import concourse.mybir as mybir
from concourse.bass_utils import run_bass_kernel_spmd

F32 = mybir.dt.float32
BF16 = mybir.dt.bfloat16
I32 = mybir.dt.int32
AF = mybir.ActivationFunctionType
ALU = mybir.AluOpType
AX = mybir.AxisListType

ENGINES = ("tensor", "vector", "scalar", "gpsimd", "sync")

D = 1024
DT = 8
T = 2048
TT = 16
HM = 2816
NE = 8
HQ_TILES = [(0, 5), (5, 10), (10, 14), (14, 18), (18, 22)]
MAXCH = 5
NH = 4
SEQ = 8192
NTQ = 64
GROUPS = [[0, 1, 2, 3], [4, 5, 6, 7]]

EXPERTS = list(range(NE))
HEADS = list(range(NH))
DO_FFN1 = True
DO_ATTN = True
DO_FFN2 = True

FX = 0
FCW = 16384
FLN = 16512
FG = 16544
FM = 16672
FC = 16800
FRS = 16816
FRF = 17392
FCF = 17904
F32_COLS = 18416

BZT = 0
AW = 16384
AW_SZ = 45400
AW_WS = [AW + 0, AW + 15360]
AW_HP = AW + 30720
AW_SI = [AW + 40960, AW + 41472]
AW_IP = AW + 0
AW_QE = AW + 24576
AW_AS = [AW + 0, AW + 20544]
AW_VT = AW + 41088
AW_PT = [AW + 42112, AW + 43136]
AW_OK = [AW + 44160, AW + 44672]
AW_OT = AW + 20544
AW_OP = AW + 36928
PS = AW + AW_SZ
B_MASK = PS
B_ID = PS + 2048
B_WG = PS + 2176
B_ZTI = [PS + 2240, PS + 3264]
B_MROW = PS + 4288
B_LNS = PS + 4544
B_ONE = PS + 5568
B_RCP = PS + 5632
BF_COLS = PS + 6144


class Prog:

    def __init__(self, nc):
        self.nc = nc
        self.items = {e: [] for e in ENGINES}
        self.sems = {}
        self.cnt = {}
        self.wm = {e: {} for e in ENGINES}
        self.regions = {}
        self.bufw = {}
        self.bufr = {}
        self.fences = {}
        self.stack = contextlib.ExitStack()

    def sem(self, name):
        if name not in self.sems:
            self.sems[name] = self.stack.enter_context(self.nc.semaphore(name))
            self.cnt[name] = 0
        return self.sems[name]

    def region(self, name, arena, lo, hi):
        self.regions[name] = (arena, lo, hi)

    def _overlaps(self, a, b):
        ra, rb = self.regions.get(a), self.regions.get(b)
        if ra is None or rb is None:
            return a == b
        return ra[0] == rb[0] and ra[1] < rb[2] and rb[1] < ra[2]

    DMA_SEMS = {"dsync": "sync", "dgp": "gpsimd"}

    def _fence(self, dmasem, val):
        q = self.DMA_SEMS[dmasem]
        fname = dmasem + "_f"
        self.sem(fname)
        for upto, fval in self.fences.get(dmasem, ()):
            if upto >= val:
                return fname, fval
        upto = self.cnt[dmasem]
        assert upto >= val
        sh, fh = self.sems[dmasem], self.sems[fname]
        self.cnt[fname] += 1
        fval = self.cnt[fname]

        def fencefn(e, sh=sh, upto=upto):
            e.wait_ge(sh, upto)
            return e.nop()
        self.items[q].append(((), fencefn, (fh, 1)))
        self.wm[q][dmasem] = upto
        self.fences.setdefault(dmasem, []).append((upto, fval))
        return fname, fval

    def op(self, eng, fn, reads=(), writes=(), sem=None, inc=1):
        semname = sem or {"tensor": "pe", "vector": "dve", "scalar": "act",
                          "gpsimd": "pool", "sync": "dsync"}[eng]
        self.sem(semname)
        is_dma = semname in self.DMA_SEMS
        need = {}

        def add(s, v, peng=None, waw_dma_ok=False):
            if v is None or v <= 0:
                return
            if peng == "tensor" and eng == "tensor":
                return
            if s in self.DMA_SEMS:
                if waw_dma_ok and is_dma and semname == s:
                    return
                s, v = self._fence(s, v)
            if self.wm[eng].get(s, 0) >= v:
                return
            if need.get(s, 0) < v:
                need[s] = v

        for b in reads:
            w = self.bufw.get(b)
            if w:
                add(w[0], w[1], w[2])
        for b in writes:
            for other in list(self.bufw.keys() | self.bufr.keys()):
                if other == b or self._overlaps(b, other):
                    w = self.bufw.get(other)
                    if w:
                        add(w[0], w[1], w[2], waw_dma_ok=True)
                    for s, v in self.bufr.get(other, {}).items():
                        add(s, v)
                    if other != b:
                        self.bufw.pop(other, None)
                        self.bufr.pop(other, None)
        waits = sorted(need.items())
        for s, v in waits:
            self.wm[eng][s] = max(self.wm[eng].get(s, 0), v)
        val = None
        if inc:
            self.cnt[semname] += inc
            val = self.cnt[semname]
        self.items[eng].append((tuple((self.sems[s], v) for s, v in waits), fn,
                                (self.sems[semname], inc) if inc else None))
        for b in writes:
            self.bufw[b] = (semname, val, eng)
            self.bufr[b] = {}
        for b in reads:
            d = self.bufr.setdefault(b, {})
            if inc and d.get(semname, 0) < val:
                d[semname] = val
        return val

    def emit(self, prologues=None):
        with self.nc.Block() as block:
            for ename in ENGINES:
                items = self.items[ename]
                pro = (prologues or {}).get(ename)
                if not items and not pro:
                    continue

                def make(items, pro):
                    def body(eng):
                        if pro:
                            pro(eng)
                        for idx, (waits, fn, inc) in enumerate(items):
                            for s, v in waits:
                                eng.wait_ge(s, v)
                            try:
                                inst = fn(eng)
                            except Exception:
                                import sys
                                print(f"EMIT FAIL eng={eng.engine} idx={idx} "
                                      f"waits={[(str(s), v) for s, v in waits]}",
                                      file=sys.stderr)
                                raise
                            if inc is not None and inst is not None:
                                inst.then_inc(inc[0], inc[1])
                    return body

                getattr(block, ename)(make(items, pro))


def build():
    nc = bass.Bass("TRN2", target_bir_lowering=False)

    x_in = nc.dram_tensor("x_in", [T, D], F32, kind="ExternalInput")
    w1t = nc.dram_tensor("w1t", [NE * D, HM], F32, kind="ExternalInput")
    w3t = nc.dram_tensor("w3t", [NE * D, HM], F32, kind="ExternalInput")
    w2t = nc.dram_tensor("w2t", [NE * HM, D], F32, kind="ExternalInput")
    wgt = nc.dram_tensor("wgt", [D, NE], F32, kind="ExternalInput")
    inpt = nc.dram_tensor("inpt", [D, 3 * D], F32, kind="ExternalInput")
    opt = nc.dram_tensor("opt", [D, D], F32, kind="ExternalInput")
    out = nc.dram_tensor("out", [T, D], F32, kind="ExternalOutput")
    qk_in = nc.dram_tensor("qk_in", [2 * D, T], BF16)
    vv_in = nc.dram_tensor("vv_in", [D, T], BF16)
    qk_out = nc.dram_tensor("qk_out", [8 * 2 * D, T], BF16, addr_space="Shared")
    vv_out = nc.dram_tensor("vv_out", [8 * D, T], BF16, addr_space="Shared")
    v_my = nc.dram_tensor("v_my", [NH * 64, SEQ], BF16)
    o_my = nc.dram_tensor("o_my", [4 * NH * 64, T], BF16)
    o_in = nc.dram_tensor("o_in", [NH * 64, SEQ], BF16)
    o_outA = nc.dram_tensor("o_outA", [8 * 2 * 64, SEQ], BF16, addr_space="Shared")
    o_outB = nc.dram_tensor("o_outB", [8 * 2 * 64, SEQ], BF16, addr_space="Shared")

    p = Prog(nc)
    ec = p.stack.enter_context
    with p.stack:
        af = ec(nc.sbuf_tensor("af", [128, F32_COLS], F32))
        ab = ec(nc.sbuf_tensor("ab", [128, BF_COLS], BF16))
        psA = ec(nc.psum_tensor("psA", [128, 1024], F32))
        psB = ec(nc.psum_tensor("psB", [128, 1024], F32))
        psC = ec(nc.psum_tensor("psC", [128, 512], F32))
        psD = ec(nc.psum_tensor("psD", [128, 512], F32))
        pst = [ec(nc.psum_tensor(f"pst{i}", [128, 1024], BF16)) for i in range(2)]
        psh = [psA[:, 0:512], psA[:, 512:1024], psB[:, 0:512], psB[:, 512:1024]]
        pshn = ["psA0", "psA1", "psB0", "psB1"]
        pscd = [psC, psD]
        pscdn = ["psC", "psD"]

        for tt in range(TT):
            p.region(f"x{tt}", "af", FX + tt * 1024, FX + (tt + 1) * 1024)
        p.region("cw", "af", FCW, FCW + 128)
        p.region("lns", "af", FLN, FLN + 32)
        p.region("gts", "af", FG, FG + 128)
        p.region("m_all0", "af", FM, FM + 64)
        p.region("m_all1", "af", FM + 64, FM + 128)
        p.region("consts", "af", FC, FC + 16)
        p.region("rscr", "af", FRS, FRS + 576)
        p.region("RfCf", "af", FRF, FRF + 1024)
        for q in range(4):
            p.region(f"zT{q}", "zTv", q * 512, (q + 1) * 512)
        for s in range(2):
            p.region(f"w13slot{s}", "ab", AW_WS[s], AW_WS[s] + 10240)
            p.region(f"w2slot{s}", "ab", AW_WS[s] + 10240, AW_WS[s] + 15360)

            p.region(f"aslot{s}", "ab", AW_AS[s], AW_AS[s] + 20544)
            p.region(f"silu{s}", "ab", AW_SI[s], AW_SI[s] + 512)
            p.region(f"PT{s}", "ab", AW_PT[s], AW_PT[s] + 1024)
            p.region(f"otok{s}", "ab", AW_OK[s], AW_OK[s] + 512)
        p.region("hp", "ab", AW_HP, AW_HP + 10240)
        p.region("ipslot", "ab", AW_IP, AW_IP + 24576)
        for i in range(4):
            p.region(f"qkev{i}", "ab", AW_QE + i * 512, AW_QE + (i + 1) * 512)
        p.region("vtscr", "ab", AW_VT, AW_VT + 1024)
        p.region("otsb", "ab", AW_OT, AW_OT + 16384)
        p.region("optsb", "ab", AW_OP, AW_OP + 8192)
        p.region("masks", "ab", B_MASK, B_MASK + 2048)
        p.region("idbf", "ab", B_ID, B_ID + 128)
        p.region("wgtsb", "ab", B_WG, B_WG + 64)
        for i in range(2):
            p.region(f"zti{i}", "ab", B_ZTI[i], B_ZTI[i] + 1024)
        p.region("mrow0", "ab", B_MROW, B_MROW + 128)
        p.region("mrow1", "ab", B_MROW + 128, B_MROW + 256)
        p.region("lnscr", "ab", B_LNS, B_LNS + 1024)
        p.region("onesb", "ab", B_ONE, B_ONE + 64)
        p.region("rcpb", "ab", B_RCP, B_RCP + 512)

        def A(base, n):
            return ab[:, base:base + n]

        def Af(base, n):
            return af[:, base:base + n]

        p.op("gpsimd", lambda e: e.iota(Af(FRF, 512), [[0, 512]], base=0,
                                        channel_multiplier=1,
                                        allow_small_or_imprecise_dtypes=True),
             writes=("RfCf",))
        p.op("gpsimd", lambda e: e.iota(Af(FCF, 512), [[1, 512]], base=0,
                                        channel_multiplier=0,
                                        allow_small_or_imprecise_dtypes=True),
             writes=("RfCf",))
        for j in range(4):
            def mk(j):
                return lambda e: e.scalar_tensor_tensor(
                    A(B_MASK + j * 512, 512), Af(FRF, 512), float(128 * j),
                    Af(FCF, 512), ALU.add, ALU.is_le)
            p.op("vector", mk(j), reads=("RfCf",), writes=("masks",))
        p.op("vector", lambda e: e.tensor_tensor(
            A(B_ID, 128), Af(FRF, 128), Af(FCF, 128), ALU.is_equal),
            reads=("RfCf",), writes=("idbf",))
        p.op("vector", lambda e: e.memset(Af(FC, 1), 1e-5), writes=("consts",))
        p.op("vector", lambda e: e.memset(Af(FC + 1, 1), -0.5), writes=("consts",))
        p.op("vector", lambda e: e.memset(Af(FC + 2, 1), 0.125), writes=("consts",))
        p.op("vector", lambda e: e.memset(Af(FC + 3, 1), -1.0), writes=("consts",))
        p.op("vector", lambda e: e.memset(ab[:, B_ONE:B_ONE + 64], 0.0),
             writes=("onesb",))
        p.op("vector", lambda e: e.memset(ab[0:1, B_ONE:B_ONE + 64], 1.0),
             writes=("onesb",))
        p.op("vector", lambda e: e.memset(ab[:, B_RCP:B_RCP + 512], 0.0),
             writes=("rcpb",))
        for dt in range(DT):
            def mk(dt):
                return lambda e: e.dma_start(out=A(B_WG + dt * 8, 8),
                                             in_=wgt[dt * 128:(dt + 1) * 128, :])
            p.op("gpsimd", mk(dt), writes=("wgtsb",), sem="dgp", inc=16)
        p.op("sync", lambda e: e.dma_start(
            out=af[:, FX:FX + 16384].rearrange("p (t d) -> p t d", d=1024),
            in_=x_in.ap().rearrange("(t p) d -> p t d", p=128)),
            writes=tuple(f"x{tt}" for tt in range(TT)), sem="dsync", inc=16)

        def zT_ap(dt, c0, n):
            return A(BZT + dt * 2048 + c0, n)

        def emit_ln(tag, gate=True):
            for tt in range(TT):
                par = tt % 2
                g = FLN + par * 16
                xs = Af(FX + tt * 1024, 1024)
                p.op("vector", lambda e, xs=xs, g=g: e.tensor_reduce(
                    Af(g, 1), xs, AX.X, ALU.add), reads=(f"x{tt}",), writes=("lns",))
                p.op("scalar", lambda e, xs=xs, g=g: e.activation(
                    A(B_LNS, 1024), xs, AF.Square, accum_out=Af(g + 1, 1)),
                    reads=(f"x{tt}",), writes=("lnscr", "lns"))
                p.op("vector", lambda e, g=g: e.tensor_scalar(
                    Af(g + 2, 1), Af(g, 1), 1.0 / D, None, ALU.mult),
                    reads=("lns",), writes=("lns",))
                p.op("vector", lambda e, g=g: e.tensor_scalar(
                    Af(g + 3, 1), Af(g + 1, 1), 1.0 / D, None, ALU.mult),
                    reads=("lns",), writes=("lns",))
                p.op("vector", lambda e, g=g: e.tensor_tensor(
                    Af(g + 4, 1), Af(g + 2, 1), Af(g + 2, 1), ALU.mult),
                    reads=("lns",), writes=("lns",))
                p.op("vector", lambda e, g=g: e.tensor_tensor(
                    Af(g + 5, 1), Af(g + 3, 1), Af(g + 4, 1), ALU.subtract),
                    reads=("lns",), writes=("lns",))
                p.op("scalar", lambda e, g=g: e.activation(
                    Af(g + 6, 1), Af(g + 5, 1), AF.Ln, bias=Af(FC, 1)),
                    reads=("lns", "consts"), writes=("lns",))
                p.op("scalar", lambda e, g=g: e.activation(
                    Af(g + 7, 1), Af(g + 6, 1), AF.Exp, scale=Af(FC + 1, 1)),
                    reads=("lns", "consts"), writes=("lns",))
                zt = A(B_ZTI[par], 1024)
                p.op("vector", lambda e, xs=xs, g=g, zt=zt: e.tensor_scalar(
                    zt, xs, Af(g + 2, 1), Af(g + 7, 1), ALU.subtract, ALU.mult),
                    reads=(f"x{tt}", "lns"), writes=(f"zti{par}",))

                def trans(e, zt=zt, par=par):
                    for dt in range(DT):
                        inst = e.transpose(pst[par][:, dt * 128:(dt + 1) * 128],
                                           zt[:, dt * 128:(dt + 1) * 128], A(B_ID, 128))
                    return inst
                p.op("tensor", trans, reads=(f"zti{par}", "idbf"), writes=(f"pst{par}",))

                def evac(e, par=par, tt=tt):
                    dst = ab[:, BZT:BZT + 16384].rearrange(
                        "p (d t) -> p d t", t=2048)[:, :, tt * 128:(tt + 1) * 128]
                    src = pst[par][:, :].rearrange("p (d t) -> p d t", t=128)
                    return e.copy(dst, src)
                p.op("scalar", evac, reads=(f"pst{par}",), writes=(f"zT{tt // 4}",))
                if gate:
                    emit_gate_tile(tt)

        def emit_gate_tile(tt):
            par = tt % 2
            b = FG + par * 64

            def mmg(e, tt=tt):
                for dt in range(DT):
                    inst = e.matmul(psC[:, 0:8], zT_ap(dt, tt * 128, 128),
                                    A(B_WG + dt * 8, 8), start=(dt == 0), stop=(dt == DT - 1))
                return inst
            p.op("tensor", mmg, reads=(f"zT{tt // 4}", "wgtsb"), writes=("psC",))
            P = psC[:, 0:8]
            p.op("vector", lambda e: e.reduce_max(Af(b, 1), P, axis=AX.X),
                 reads=("psC",), writes=("gts",))
            p.op("vector", lambda e: e.tensor_scalar(
                Af(b + 1, 1), Af(b, 1), -1.0, None, ALU.mult), reads=("gts",), writes=("gts",))
            p.op("scalar", lambda e: e.activation(
                Af(b + 16, 8), P, AF.Exp, bias=Af(b + 1, 1)),
                reads=("psC", "gts"), writes=("gts",))
            p.op("vector", lambda e: e.tensor_reduce(
                Af(b + 2, 1), Af(b + 16, 8), AX.X, ALU.add), reads=("gts",), writes=("gts",))
            p.op("vector", lambda e: e.reciprocal(Af(b + 3, 1), Af(b + 2, 1)),
                 reads=("gts",), writes=("gts",))
            p.op("vector", lambda e: e.tensor_scalar(
                Af(b + 24, 8), Af(b + 16, 8), Af(b + 3, 1), None, ALU.mult),
                reads=("gts",), writes=("gts",))
            p.op("vector", lambda e: e.reduce_max(Af(b + 4, 1), Af(b + 24, 8), axis=AX.X),
                 reads=("gts",), writes=("gts",))
            p.op("vector", lambda e: e.tensor_scalar(
                Af(b + 32, 8), Af(b + 24, 8), Af(b + 4, 1), None, ALU.is_ge),
                reads=("gts",), writes=("gts",))
            p.op("vector", lambda e: e.scalar_tensor_tensor(
                Af(b + 40, 8), Af(b + 32, 8), -1e30, Af(b + 24, 8), ALU.mult, ALU.add),
                reads=("gts",), writes=("gts",))
            p.op("vector", lambda e: e.reduce_max(Af(b + 5, 1), Af(b + 40, 8), axis=AX.X),
                 reads=("gts",), writes=("gts",))
            p.op("vector", lambda e: e.tensor_scalar(
                Af(b + 48, 8), Af(b + 40, 8), Af(b + 5, 1), None, ALU.is_equal),
                reads=("gts",), writes=("gts",))
            p.op("vector", lambda e: e.tensor_tensor(
                Af(b + 6, 1), Af(b + 4, 1), Af(b + 5, 1), ALU.add),
                reads=("gts",), writes=("gts",))
            p.op("vector", lambda e: e.tensor_scalar(
                Af(b + 7, 1), Af(b + 6, 1), 1e-20, None, ALU.add),
                reads=("gts",), writes=("gts",))
            p.op("vector", lambda e: e.reciprocal(Af(b + 8, 1), Af(b + 7, 1)),
                 reads=("gts",), writes=("gts",))
            p.op("vector", lambda e: e.tensor_scalar(
                Af(b + 9, 1), Af(b + 4, 1), Af(b + 8, 1), 0.5, ALU.mult, ALU.mult),
                reads=("gts",), writes=("gts",))
            p.op("vector", lambda e: e.tensor_scalar(
                Af(b + 10, 1), Af(b + 5, 1), Af(b + 8, 1), 0.5, ALU.mult, ALU.mult),
                reads=("gts",), writes=("gts",))
            p.op("vector", lambda e: e.tensor_scalar(
                Af(b + 56, 8), Af(b + 48, 8), Af(b + 10, 1), None, ALU.mult),
                reads=("gts",), writes=("gts",))
            p.op("vector", lambda e, tt=tt: e.scalar_tensor_tensor(
                Af(FCW + tt * 8, 8), Af(b + 32, 8), Af(b + 9, 1), Af(b + 56, 8),
                ALU.mult, ALU.add), reads=("gts",), writes=("cw",))

        def load_wchunk(e_idx, hq, slot):
            h0, h1 = HQ_TILES[hq]
            nt = h1 - h0
            hs, hw = h0 * 128, nt * 128
            base = AW_WS[slot]

            def mk13(src, boff, hs=hs, hw=hw, base=base, e_idx=e_idx):
                def f(e):
                    dst = ab[:, base + boff:base + boff + 8 * 640].rearrange(
                        "p (dt h) -> p dt h", h=640)[:, :, 0:hw]
                    return e.dma_start(
                        out=dst,
                        in_=bass.AP(src, e_idx * D * HM + hs,
                                    [[HM, 128], [128 * HM, 8], [1, hw]]))
                return f
            p.op("gpsimd", mk13(w1t, 0), writes=(f"w13slot{slot}",), sem="dgp", inc=16)
            p.op("gpsimd", mk13(w3t, 5120), writes=(f"w13slot{slot}",), sem="dgp", inc=16)

            def mk2(h0=h0, nt=nt, base=base, e_idx=e_idx):
                def f(e):
                    dst = ab[:, base + 10240:base + 10240 + nt * 1024].rearrange(
                        "p (t d) -> p t d", d=1024)
                    return e.dma_start(
                        out=dst,
                        in_=bass.AP(w2t, (e_idx * HM + h0 * 128) * D,
                                    [[D, 128], [128 * D, nt], [1, D]]))
                return f
            p.op("gpsimd", mk2(), writes=(f"w2slot{slot}",), sem="dgp", inc=16)

        def emit_ffn(tag, skip_first_load=False):
            units = [(ei, hq) for ei in EXPERTS for hq in range(len(HQ_TILES))]
            if not skip_first_load:
                load_wchunk(units[0][0], units[0][1], 0)
            for u, (ei, hq) in enumerate(units):
                slot = u % 2
                if u + 1 < len(units):
                    load_wchunk(units[u + 1][0], units[u + 1][1], (u + 1) % 2)
                h0, h1 = HQ_TILES[hq]
                nt = h1 - h0
                base = AW_WS[slot]
                for htl in range(nt):
                    for tch in range(4):
                        rot = (htl * 4 + tch) % 2

                        def mmA(e, base=base, htl=htl, tch=tch, rot=rot, woff=0):
                            for dt in range(DT):
                                inst = e.matmul(
                                    psh[rot],
                                    A(base + woff + dt * 640 + htl * 128, 128),
                                    zT_ap(dt, tch * 512, 512),
                                    start=(dt == 0), stop=(dt == DT - 1))
                            return inst
                        p.op("tensor", lambda e, f=mmA: f(e),
                             reads=(f"w13slot{slot}", f"zT{tch}"), writes=(pshn[rot],))

                        def mmB(e, base=base, htl=htl, tch=tch, rot=rot):
                            for dt in range(DT):
                                inst = e.matmul(
                                    psh[2 + rot],
                                    A(base + 5120 + dt * 640 + htl * 128, 128),
                                    zT_ap(dt, tch * 512, 512),
                                    start=(dt == 0), stop=(dt == DT - 1))
                            return inst
                        p.op("tensor", mmB, reads=(f"w13slot{slot}", f"zT{tch}"),
                             writes=(pshn[2 + rot],))
                        p.op("scalar", lambda e, rot=rot: e.activation(
                            A(AW_SI[rot], 512), psh[rot], AF.Silu),
                            reads=(pshn[rot],), writes=(f"silu{rot}",))
                        p.op("vector", lambda e, rot=rot, htl=htl, tch=tch: e.tensor_tensor(
                            A(AW_HP + htl * 2048 + tch * 512, 512),
                            A(AW_SI[rot], 512), psh[2 + rot], ALU.mult),
                            reads=(f"silu{rot}", pshn[2 + rot]), writes=("hp",))
                for tt in range(TT):
                    for dh in range(2):
                        rot = (tt * 2 + dh) % 2

                        def mmW(e, base=base, nt=nt, tt=tt, dh=dh, rot=rot):
                            for htl in range(nt):
                                inst = e.matmul(
                                    pscd[rot][:, :],
                                    A(AW_HP + htl * 2048 + tt * 128, 128),
                                    A(base + 10240 + htl * 1024 + dh * 512, 512),
                                    start=(htl == 0), stop=(htl == nt - 1))
                            return inst
                        p.op("tensor", mmW, reads=("hp", f"w2slot{slot}"),
                             writes=(pscdn[rot],))
                        p.op("vector", lambda e, tt=tt, dh=dh, rot=rot, ei=ei:
                             e.scalar_tensor_tensor(
                                 Af(FX + tt * 1024 + dh * 512, 512),
                                 pscd[rot][:, :], Af(FCW + tt * 8 + ei, 1),
                                 Af(FX + tt * 1024 + dh * 512, 512), ALU.mult, ALU.add),
                             reads=(pscdn[rot], "cw", f"x{tt}"), writes=(f"x{tt}",))

        def emit_qkv(regs):
            nfc = 12
            for half in range(2):
                def mkip(half=half):
                    def f(e):
                        dst = ab[:, AW_IP:AW_IP + 24576].rearrange(
                            "p (dt f) -> p dt f", f=3072)[:, :, half * 1536:(half + 1) * 1536]
                        return e.dma_start(
                            out=dst,
                            in_=bass.AP(inpt, half * 1536,
                                        [[3 * D, 128], [128 * 3 * D, 8], [1, 1536]]))
                    return f
                p.op("gpsimd", mkip(), writes=("ipslot",), sem="dgp", inc=16)
            for fc in range(nfc):
                for ft in range(2):
                    for tch in range(4):
                        rot = (ft * 4 + tch) % 4
                        f0 = fc * 256 + ft * 128

                        def mmQ(e, fc=fc, ft=ft, tch=tch, rot=rot):
                            for dt in range(DT):
                                inst = e.matmul(
                                    psh[rot],
                                    A(AW_IP + dt * 3072 + fc * 256 + ft * 128, 128),
                                    zT_ap(dt, tch * 512, 512),
                                    start=(dt == 0), stop=(dt == DT - 1))
                            return inst
                        p.op("tensor", mmQ, reads=("ipslot", f"zT{tch}"),
                             writes=(pshn[rot],))
                        ev = rot
                        if f0 < D:
                            p.op("scalar", lambda e, rot=rot, ev=ev: e.activation(
                                A(AW_QE + ev * 512, 512), psh[rot], AF.Copy,
                                scale=Af(FC + 2, 1)),
                                reads=(pshn[rot], "consts"), writes=(f"qkev{ev}",))
                        else:
                            p.op("scalar", lambda e, rot=rot, ev=ev: e.copy(
                                A(AW_QE + ev * 512, 512), psh[rot]),
                                reads=(pshn[rot],), writes=(f"qkev{ev}",))
                        if f0 < 2 * D:
                            p.op("sync", lambda e, f0=f0, tch=tch, ev=ev: e.dma_start(
                                out=qk_in[f0:f0 + 128, tch * 512:(tch + 1) * 512],
                                in_=A(AW_QE + ev * 512, 512)),
                                reads=(f"qkev{ev}",), writes=("qk_in",),
                                sem="dsync", inc=16)
                        else:
                            p.op("sync", lambda e, f0=f0, tch=tch, ev=ev: e.dma_start(
                                out=vv_in[f0 - 2 * D:f0 - 2 * D + 128,
                                          tch * 512:(tch + 1) * 512],
                                in_=A(AW_QE + ev * 512, 512)),
                                reads=(f"qkev{ev}",), writes=("vv_in",),
                                sem="dsync", inc=16)
                if fc == 7:
                    def cc1a(e):
                        return e.collective_compute(
                            "AllGather", ALU.bypass,
                            replica_groups=[list(range(8))],
                            ins=[qk_in.ap().opt()], outs=[qk_out.ap().opt()])
                    p.op("gpsimd", cc1a, reads=("qk_in",), writes=("qk_out",),
                         sem="cc1a", inc=1)

            def cc1b(e):
                return e.collective_compute(
                    "AllGather", ALU.bypass, replica_groups=[list(range(8))],
                    ins=[vv_in.ap().opt()], outs=[vv_out.ap().opt()])
            p.op("gpsimd", cc1b, reads=("vv_in",), writes=("vv_out",), sem="cc1b", inc=1)

            def vstage(e):
                return e.dma_start(
                    out=v_my.ap().rearrange("r (i t) -> r i t", i=4),
                    in_=bass.AP(vv_out, regs["rvv"],
                                [[T, 256], [1024 * T, 4], [1, T]]))
            p.op("sync", vstage, reads=("vv_out",), writes=("v_my",),
                 sem="dsync", inc=16)

        def kcount(tqt):
            return tqt // 4 + 1

        def prep_ops(h, regs):
            sl = h % 2
            par = h % 2
            qb, kb, vb = AW_AS[sl], AW_AS[sl] + 8192, AW_AS[sl] + 16384
            aslot = f"aslot{sl}"
            fmh = FM + par * 64
            mrb = B_MROW + par * 128
            mname = f"m_all{par}"
            rname = f"mrow{par}"
            ops_io = []
            ops = ops_io
            ops.append(lambda kb=kb, aslot=aslot: p.op(
                "vector", lambda e: e.memset(ab[64:65, kb:kb + 8192], 1.0),
                writes=(aslot,)))
            ops.append(lambda vb=vb, aslot=aslot: p.op(
                "vector", lambda e: e.memset(
                    ab[:, vb:vb + 4160].rearrange(
                        "p (t c) -> p t c", c=65)[:, :, 64:65], 1.0),
                writes=(aslot,)))

            def mkq(e, h=h, qb=qb):
                e.reg_add(regs["scr"], regs["rqk"], h * 64 * T)
                return e.dma_start(
                    out=ab[0:64, qb:qb + 8192].rearrange("p (i t) -> p i t", i=4),
                    in_=bass.AP(qk_out, regs["scr"],
                                [[T, 64], [2048 * T, 4], [1, T]]))
            ops.append(lambda f=mkq, aslot=aslot: p.op(
                "sync", f, reads=("qk_out",), writes=(aslot,), sem="dsync", inc=16))

            def mkk(e, h=h, kb=kb):
                e.reg_add(regs["scr"], regs["rqk"], (1024 + h * 64) * T)
                return e.dma_start(
                    out=ab[0:64, kb:kb + 8192].rearrange("p (i t) -> p i t", i=4),
                    in_=bass.AP(qk_out, regs["scr"],
                                [[T, 64], [2048 * T, 4], [1, T]]))
            ops.append(lambda f=mkk, aslot=aslot: p.op(
                "sync", f, reads=("qk_out",), writes=(aslot,), sem="dsync", inc=16))

            for i in range(4):
                for half in range(2):
                    c0 = i * 2048 + half * 1024

                    def mkv(e, h=h, c0=c0):
                        return e.dma_start(
                            out=ab[0:64, AW_VT:AW_VT + 1024],
                            in_=v_my[h * 64:(h + 1) * 64, c0:c0 + 1024])
                    ops.append(lambda f=mkv: p.op(
                        "sync", f, reads=("v_my",), writes=("vtscr",),
                        sem="dsync", inc=16))

                    def vtr(e, half=half):
                        for j in range(8):
                            inst = e.transpose(
                                pst[half][:, j * 128:(j + 1) * 128],
                                ab[0:128, AW_VT + j * 128:AW_VT + (j + 1) * 128],
                                A(B_ID, 128))
                        return inst
                    ops.append(lambda f=vtr, half=half: p.op(
                        "tensor", f, reads=("vtscr", "idbf"), writes=(f"pst{half}",)))

                    def vev(e, half=half, c0=c0, vb=vb):
                        dst = ab[:, vb:vb + 4160].rearrange(
                            "p (t c) -> p t c", c=65)[:, c0 // 128:c0 // 128 + 8, 0:64]
                        src = pst[half][:, 0:1024].rearrange(
                            "p (t c) -> p t c", c=128)[:, :, 0:64]
                        return e.copy(dst, src)
                    ops.append(lambda f=vev, half=half, aslot=aslot: p.op(
                        "scalar", f, reads=(f"pst{half}",), writes=(aslot,)))

            ops_p1 = []
            ops = ops_p1
            ops.append(lambda qb=qb, aslot=aslot: p.op(
                "vector", lambda e: e.memset(ab[64:65, qb:qb + 8192], 0.0),
                writes=(aslot,)))
            for tqt in range(NTQ):
                kend = (tqt + 1) * 128
                k0 = max(0, kend - 512)
                nsamp = kend - k0
                X = psA if tqt % 2 == 0 else psB
                Xn = ("psA0", "psA1") if tqt % 2 == 0 else ("psB0", "psB1")

                def mmS(e, tqt=tqt, k0=k0, nsamp=nsamp, X=X, qb=qb, kb=kb):
                    return e.matmul(X[:, 0:nsamp],
                                    ab[0:65, qb + tqt * 128:qb + (tqt + 1) * 128],
                                    ab[0:65, kb + k0:kb + k0 + nsamp],
                                    start=True, stop=True)
                ops.append(lambda f=mmS, Xn=Xn, aslot=aslot: p.op(
                    "tensor", f, reads=(aslot,), writes=Xn))

                def red(e, tqt=tqt, nsamp=nsamp, X=X, fmh=fmh):
                    return e.reduce_max(Af(fmh + tqt, 1), X[:, 0:nsamp], axis=AX.X)
                ops.append(lambda f=red, Xn=Xn, mname=mname: p.op(
                    "vector", f, reads=Xn, writes=(mname,)))

            ops.append(lambda fmh=fmh, mrb=mrb, mname=mname, rname=rname: p.op(
                "vector", lambda e: e.tensor_scalar(
                    A(mrb, 64), Af(fmh, 64), -1.0, None, ALU.mult),
                reads=(mname,), writes=(rname,)))
            ops.append(lambda mrb=mrb, par=par, rname=rname: p.op(
                "tensor", lambda e: e.transpose(
                    pst[par][0:64, 0:128], A(mrb, 64), A(B_ID, 128)),
                reads=(rname, "idbf"), writes=(f"pst{par}",)))
            ops.append(lambda mrb=mrb, par=par, rname=rname: p.op(
                "scalar", lambda e: e.copy(ab[0:64, mrb:mrb + 128],
                                           pst[par][0:64, 0:128]),
                reads=(f"pst{par}",), writes=(rname,)))
            ops.append(lambda mrb=mrb, qb=qb, rname=rname, aslot=aslot: p.op(
                "sync", lambda e: e.dma_start(
                    out=ab[64:65, qb:qb + 8192],
                    in_=ab[0:64, mrb:mrb + 128]),
                reads=(rname,), writes=(aslot,), sem="dsync", inc=16))
            return ops_io, ops_p1

        def emit_phase2(h, regs, inject=()):
            sl = h % 2
            qb, kb, vb = AW_AS[sl], AW_AS[sl] + 8192, AW_AS[sl] + 16384
            aslot = f"aslot{sl}"
            inject = list(inject)
            inj_i = [0]
            pend_av = [None]
            pend_norm = [None]

            def emit_norm(tqc):
                av = tqc % 2
                avb, avn = pscd[av], pscdn[av]
                ok = tqc % 2
                p.op("scalar", lambda e, avb=avb: e.activation(
                    af[0:1, FRS:FRS + 512], avb[64:65, :], AF.Ln),
                    reads=(avn,), writes=("rscr",))

                def expf(e):
                    with nc.allow_low_precision(reason="softmax denom 1/l in bf16"):
                        return e.activation(ab[0:1, B_RCP:B_RCP + 512],
                                            af[0:1, FRS:FRS + 512], AF.Exp,
                                            scale=af[0:1, FC + 3:FC + 4])
                p.op("scalar", expf, reads=("rscr", "consts"), writes=("rcpb",))
                p.op("scalar", lambda e, avb=avb, ok=ok: e.copy(
                    ab[0:64, AW_OK[ok]:AW_OK[ok] + 512], avb[0:64, :]),
                    reads=(avn,), writes=(f"otok{ok}",))

                def bcast(e, avb=avb):
                    return e.matmul(avb[0:64, :], ab[:, B_ONE:B_ONE + 64],
                                    ab[:, B_RCP:B_RCP + 512], start=True, stop=True)
                p.op("tensor", bcast, reads=("rcpb", "onesb"), writes=(avn,))
                p.op("vector", lambda e, avb=avb, ok=ok: e.tensor_tensor(
                    ab[0:64, AW_OK[ok]:AW_OK[ok] + 512],
                    ab[0:64, AW_OK[ok]:AW_OK[ok] + 512],
                    avb[0:64, :], ALU.mult),
                    reads=(avn, f"otok{ok}"), writes=(f"otok{ok}",))
                p.op("sync", lambda e, h=h, tqc=tqc, ok=ok: e.dma_start(
                    out=o_in[h * 64:(h + 1) * 64, tqc * 512:(tqc + 1) * 512],
                    in_=ab[0:64, AW_OK[ok]:AW_OK[ok] + 512]),
                    reads=(f"otok{ok}",), writes=("o_in",), sem="dsync", inc=16)

            for tqc in range(16):
                av = tqc % 2
                avb, avn = pscd[av], pscdn[av]
                ntk = (tqc + 1) * 4
                for tp in range(ntk // 2):
                    X = psA if tp % 2 == 0 else psB
                    Xn = ("psA0", "psA1") if tp % 2 == 0 else ("psB0", "psB1")
                    pt = tp % 2

                    def mmT2(e, tp=tp, tqc=tqc, X=X, qb=qb, kb=kb):
                        for half in range(2):
                            tkt = 2 * tp + half
                            inst = e.matmul(
                                X[:, half * 512:(half + 1) * 512],
                                ab[0:65, kb + tkt * 128:kb + (tkt + 1) * 128],
                                ab[0:65, qb + tqc * 512:qb + (tqc + 1) * 512],
                                start=True, stop=True)
                        return inst
                    p.op("tensor", mmT2, reads=(aslot,), writes=Xn)
                    p.op("scalar", lambda e, X=X, pt=pt: e.activation(
                        A(AW_PT[pt], 1024), X[:, :], AF.Exp),
                        reads=Xn, writes=(f"PT{pt}",))
                    j0 = 2 * tp - 4 * tqc
                    if 0 <= j0 <= 3:
                        p.op("vector", lambda e, pt=pt, j0=j0: e.tensor_tensor(
                            A(AW_PT[pt], 1024), A(AW_PT[pt], 1024),
                            A(B_MASK + j0 * 512, 1024), ALU.mult),
                            reads=(f"PT{pt}", "masks"), writes=(f"PT{pt}",))
                    if pend_av[0] is not None:
                        fn, rd, wr = pend_av[0]
                        p.op("tensor", fn, reads=rd, writes=wr)
                        pend_av[0] = None
                    if pend_norm[0] is not None:
                        emit_norm(pend_norm[0])
                        pend_norm[0] = None
                    if inj_i[0] < len(inject):
                        inject[inj_i[0]]()
                        inj_i[0] += 1

                    def mmAV(e, tp=tp, pt=pt, avb=avb, vb=vb):
                        for half in range(2):
                            tkt = 2 * tp + half
                            inst = e.matmul(
                                avb[0:65, :],
                                A(vb + tkt * 65, 65),
                                A(AW_PT[pt] + half * 512, 512),
                                start=(tp == 0 and half == 0), stop=False,
                                skip_group_check=True)
                        return inst
                    pend_av[0] = (mmAV, (f"PT{pt}", aslot), (avn,))
                fn, rd, wr = pend_av[0]
                p.op("tensor", fn, reads=rd, writes=wr)
                pend_av[0] = None
                if pend_norm[0] is not None:
                    emit_norm(pend_norm[0])
                pend_norm[0] = tqc
            emit_norm(pend_norm[0])
            while inj_i[0] < len(inject):
                inject[inj_i[0]]()
                inj_i[0] += 1

        def emit_attn(regs):
            p0io, p0p1 = prep_ops(HEADS[0], regs)
            for cl in p0io + p0p1:
                cl()
            for idx, h in enumerate(HEADS):
                if idx + 1 < len(HEADS):
                    nio, np1 = prep_ops(HEADS[idx + 1], regs)
                else:
                    nio, np1 = [], []
                emit_phase2(h, regs, inject=nio)
                for cl in np1:
                    cl()
                if h == 1:
                    def cc3a(e):
                        return e.collective_compute(
                            "AllGather", ALU.bypass,
                            replica_groups=[list(range(8))],
                            ins=[o_in[0:128, :].opt()], outs=[o_outA.ap().opt()])
                    p.op("gpsimd", cc3a, reads=("o_in",), writes=("o_outA",),
                         sem="cc3a", inc=1)
                if h == 2:
                    def cc3b1(e):
                        return e.collective_compute(
                            "AllGather", ALU.bypass,
                            replica_groups=[list(range(8))],
                            ins=[o_in[128:192, :].opt()], outs=[o_outB[0:512, :].opt()])
                    p.op("gpsimd", cc3b1, reads=("o_in",), writes=("o_outB1",),
                         sem="cc3b1", inc=1)
            if DO_FFN2:
                load_wchunk(EXPERTS[0], 0, 0)

            def cc3b2(e):
                return e.collective_compute(
                    "AllGather", ALU.bypass, replica_groups=[list(range(8))],
                    ins=[o_in[192:256, :].opt()], outs=[o_outB[512:1024, :].opt()])
            p.op("gpsimd", cc3b2, reads=("o_in",), writes=("o_outB2",), sem="cc3b2", inc=1)
            for et in range(DT):
                def mko(et=et):
                    return lambda e: e.dma_start(
                        out=A(AW_OP + et * 1024, 1024),
                        in_=opt[et * 128:(et + 1) * 128, :])
                p.op("gpsimd", mko(), writes=("optsb",), sem="dgp", inc=16)
            def ostageA(e):
                dst = o_my.ap().rearrange("(j x) t -> j x t", x=256)[:, 0:128, :]
                return e.dma_start(
                    out=dst,
                    in_=bass.AP(o_outA, regs["roth"],
                                [[128 * SEQ, 4], [SEQ, 128], [1, T]]))
            p.op("sync", ostageA, reads=("o_outA",), writes=("o_my",),
                 sem="dsync", inc=16)

            def ostageB(e):
                dst = o_my.ap().rearrange("(j x) t -> j x t", x=256)[:, 128:192, :]
                e.reg_mov(regs["scr"], 4 * 64 * SEQ)
                e.reg_mul(regs["scr"], regs["bg"], regs["scr"])
                e.reg_add(regs["scr2"], regs["scr"], regs["rmt"])
                return e.dma_start(
                    out=dst,
                    in_=bass.AP(o_outB, regs["scr2"],
                                [[64 * SEQ, 4], [SEQ, 64], [1, T]]))
            p.op("sync", ostageB, reads=("o_outB1",), writes=("o_my",),
                 sem="dsync", inc=16)

            def ostageC(e):
                dst = o_my.ap().rearrange("(j x) t -> j x t", x=256)[:, 192:256, :]
                e.reg_add(regs["scr2"], regs["scr"], regs["rmt"])
                e.reg_add(regs["scr2"], regs["scr2"], 512 * SEQ)
                return e.dma_start(
                    out=dst,
                    in_=bass.AP(o_outB, regs["scr2"],
                                [[64 * SEQ, 4], [SEQ, 64], [1, T]]))
            p.op("sync", ostageC, reads=("o_outB2",), writes=("o_my",),
                 sem="dsync", inc=16)
            for et in range(DT):
                def mkt(et=et):
                    return lambda e: e.dma_start(
                        out=A(AW_OT + et * 2048, 2048),
                        in_=o_my[et * 128:(et + 1) * 128, :])
                p.op("sync", mkt(), reads=("o_my",), writes=("otsb",),
                     sem="dsync", inc=16)
            for tt in range(TT):
                for fh in range(2):
                    rot = (tt * 2 + fh) % 4

                    def mmo(e, tt=tt, fh=fh, rot=rot):
                        for et in range(DT):
                            inst = e.matmul(
                                psh[rot],
                                A(AW_OT + et * 2048 + tt * 128, 128),
                                A(AW_OP + et * 1024 + fh * 512, 512),
                                start=(et == 0), stop=(et == DT - 1))
                        return inst
                    p.op("tensor", mmo, reads=("otsb", "optsb"), writes=(pshn[rot],))
                    p.op("vector", lambda e, tt=tt, fh=fh, rot=rot: e.tensor_tensor(
                        Af(FX + tt * 1024 + fh * 512, 512),
                        Af(FX + tt * 1024 + fh * 512, 512),
                        psh[rot], ALU.add),
                        reads=(pshn[rot], f"x{tt}"), writes=(f"x{tt}",))

        if DO_FFN1:
            emit_ln("ln1")
            emit_ffn("ffn1")
        if DO_ATTN:
            regs = {}
            emit_ln("ln2", gate=False)
            emit_qkv(regs)

            def sync_pro(e):
                regs["rank"] = e.alloc_register(name="r_rank")
                regs["bg"] = e.alloc_register(name="r_bg")
                regs["rm"] = e.alloc_register(name="r_rm")
                regs["rqk"] = e.alloc_register(name="r_rqk")
                regs["rvv"] = e.alloc_register(name="r_rvv")
                regs["roth"] = e.alloc_register(name="r_roth")
                regs["scr"] = e.alloc_register(name="r_scr")
                regs["scr2"] = e.alloc_register(name="r_scr2")
                regs["rmt"] = e.alloc_register(name="r_rmt")
                e.cc_rank_ld(regs["rank"], replica_groups=[list(range(8))])
                e.reg_mov(regs["scr"], 4)
                e.reg_div(regs["bg"], regs["rank"], regs["scr"])
                e.reg_mod(regs["rm"], regs["rank"], regs["scr"])
                e.reg_mov(regs["scr"], 4 * 2048 * T)
                e.reg_mul(regs["rqk"], regs["bg"], regs["scr"])
                e.reg_mov(regs["scr"], 4 * 64 * T)
                e.reg_mul(regs["scr"], regs["rm"], regs["scr"])
                e.reg_add(regs["rqk"], regs["rqk"], regs["scr"])
                e.reg_mov(regs["scr"], 4 * 1024 * T)
                e.reg_mul(regs["rvv"], regs["bg"], regs["scr"])
                e.reg_mov(regs["scr"], 4 * 64 * T)
                e.reg_mul(regs["scr"], regs["rm"], regs["scr"])
                e.reg_add(regs["rvv"], regs["rvv"], regs["scr"])
                e.reg_mov(regs["scr"], 4 * 128 * SEQ)
                e.reg_mul(regs["roth"], regs["bg"], regs["scr"])
                e.reg_mov(regs["scr"], T)
                e.reg_mul(regs["rmt"], regs["rm"], regs["scr"])
                e.reg_add(regs["roth"], regs["roth"], regs["rmt"])
            emit_attn(regs)
        else:
            sync_pro = None
        if DO_FFN2:
            emit_ln("ln3")
            emit_ffn("ffn2", skip_first_load=DO_ATTN)
        p.op("sync", lambda e: e.dma_start(
            out=out.ap().rearrange("(t p) d -> p t d", p=128),
            in_=af[:, FX:FX + 16384].rearrange("p (t d) -> p t d", d=1024)),
            reads=tuple(f"x{tt}" for tt in range(TT)), sem="dsync", inc=16)
        final = p.cnt["dsync"]
        p.op("sync", lambda e, final=final: e.wait_ge(p.sems["dsync"], final), inc=0)

        p.emit(prologues={"sync": sync_pro} if DO_ATTN else None)
    return nc


_NC_CACHE = None


def kernel(L0_attn_query, L0_attn_key, L0_attn_value, L0_attn_in_proj_weight,
           L0_attn_out_proj_weight, L0_ffn_Wg, L0_ffn_W1, L0_ffn_W2, L0_ffn_W3):
    global _NC_CACHE
    bfloat16 = ml_dtypes.bfloat16
    x = np.ascontiguousarray(np.asarray(L0_attn_query, np.float32))
    W1 = np.asarray(L0_ffn_W1, np.float32)
    W2 = np.asarray(L0_ffn_W2, np.float32)
    W3 = np.asarray(L0_ffn_W3, np.float32)
    Wg = np.asarray(L0_ffn_Wg, np.float32)
    ip = np.asarray(L0_attn_in_proj_weight, np.float32)
    op = np.asarray(L0_attn_out_proj_weight, np.float32)

    w1t = np.ascontiguousarray(W1.transpose(0, 2, 1)).reshape(NE * D, HM)
    w3t = np.ascontiguousarray(W3.transpose(0, 2, 1)).reshape(NE * D, HM)
    w2t = np.ascontiguousarray(W2.transpose(0, 2, 1)).reshape(NE * HM, D)
    wgt = np.ascontiguousarray(Wg.T)
    inpt = np.ascontiguousarray(ip.T)
    opt_ = np.ascontiguousarray(op.T)

    if _NC_CACHE is None:
        _NC_CACHE = build()
    nc = _NC_CACHE
    in_maps = []
    for c in range(8):
        b, ch = c // 4, c % 4
        in_maps.append(dict(
            x_in=np.ascontiguousarray(x[b, ch * T:(ch + 1) * T]),
            w1t=w1t, w3t=w3t, w2t=w2t, wgt=wgt, inpt=inpt, opt=opt_,
        ))
    res = run_bass_kernel_spmd(nc, in_maps, core_ids=list(range(8)))
    outp = np.empty((2, SEQ, D), np.float32)
    for c in range(8):
        b, ch = c // 4, c % 4
        outp[b, ch * T:(ch + 1) * T] = res.results[c]["out"]
    return outp


# version 13 (final)
# speedup vs baseline: 1.0035x; 1.0035x over previous
import contextlib
import numpy as np
import ml_dtypes
import concourse.bass as bass
import concourse.mybir as mybir
from concourse.bass_utils import run_bass_kernel_spmd

F32 = mybir.dt.float32
BF16 = mybir.dt.bfloat16
I32 = mybir.dt.int32
AF = mybir.ActivationFunctionType
ALU = mybir.AluOpType
AX = mybir.AxisListType

ENGINES = ("tensor", "vector", "scalar", "gpsimd", "sync")

D = 1024
DT = 8
T = 2048
TT = 16
HM = 2816
NE = 8
HQ_TILES = [(0, 5), (5, 10), (10, 14), (14, 18), (18, 22)]
MAXCH = 5
NH = 4
SEQ = 8192
NTQ = 64
GROUPS = [[0, 1, 2, 3], [4, 5, 6, 7]]

EXPERTS = list(range(NE))
HEADS = list(range(NH))
DO_FFN1 = True
DO_ATTN = True
DO_FFN2 = True

FX = 0
FCW = 16384
FLN = 16512
FG = 16544
FM = 16672
FC = 16800
FRS = 16816
FRF = 17392
FCF = 17904
F32_COLS = 18416

BZT = 0
AW = 16384
AW_SZ = 45400
AW_WS = [AW + 0, AW + 15360]
AW_HP = AW + 30720
AW_SI = [AW + 40960, AW + 41472]
AW_IP = AW + 0
AW_QE = AW + 24576
AW_AS = [AW + 0, AW + 20544]
AW_VT = AW + 41088
AW_PT = [AW + 42112, AW + 43136]
AW_OK = [AW + 44160, AW + 44672]
AW_OT = AW + 20544
AW_OP = AW + 36928
PS = AW + AW_SZ
B_MASK = PS
B_ID = PS + 2048
B_WG = PS + 2176
B_ZTI = [PS + 2240, PS + 3264]
B_MROW = PS + 4288
B_LNS = PS + 4544
B_ONE = PS + 5568
B_RCP = PS + 5632
BF_COLS = PS + 6144


class Prog:

    def __init__(self, nc):
        self.nc = nc
        self.items = {e: [] for e in ENGINES}
        self.sems = {}
        self.cnt = {}
        self.wm = {e: {} for e in ENGINES}
        self.regions = {}
        self.bufw = {}
        self.bufr = {}
        self.fences = {}
        self.stack = contextlib.ExitStack()

    def sem(self, name):
        if name not in self.sems:
            self.sems[name] = self.stack.enter_context(self.nc.semaphore(name))
            self.cnt[name] = 0
        return self.sems[name]

    def region(self, name, arena, lo, hi):
        self.regions[name] = (arena, lo, hi)

    def _overlaps(self, a, b):
        ra, rb = self.regions.get(a), self.regions.get(b)
        if ra is None or rb is None:
            return a == b
        return ra[0] == rb[0] and ra[1] < rb[2] and rb[1] < ra[2]

    DMA_SEMS = {"dsync": "sync", "dgp": "gpsimd"}

    def _fence(self, dmasem, val):
        q = self.DMA_SEMS[dmasem]
        fname = dmasem + "_f"
        self.sem(fname)
        for upto, fval in self.fences.get(dmasem, ()):
            if upto >= val:
                return fname, fval
        upto = self.cnt[dmasem]
        assert upto >= val
        sh, fh = self.sems[dmasem], self.sems[fname]
        self.cnt[fname] += 1
        fval = self.cnt[fname]

        def fencefn(e, sh=sh, upto=upto):
            e.wait_ge(sh, upto)
            return e.nop()
        self.items[q].append(((), fencefn, (fh, 1)))
        self.wm[q][dmasem] = upto
        self.fences.setdefault(dmasem, []).append((upto, fval))
        return fname, fval

    def op(self, eng, fn, reads=(), writes=(), sem=None, inc=1):
        semname = sem or {"tensor": "pe", "vector": "dve", "scalar": "act",
                          "gpsimd": "pool", "sync": "dsync"}[eng]
        self.sem(semname)
        is_dma = semname in self.DMA_SEMS
        need = {}

        def add(s, v, peng=None, waw_dma_ok=False):
            if v is None or v <= 0:
                return
            if peng == "tensor" and eng == "tensor":
                return
            if s in self.DMA_SEMS:
                if waw_dma_ok and is_dma and semname == s:
                    return
                s, v = self._fence(s, v)
            if self.wm[eng].get(s, 0) >= v:
                return
            if need.get(s, 0) < v:
                need[s] = v

        for b in reads:
            w = self.bufw.get(b)
            if w:
                add(w[0], w[1], w[2])
        for b in writes:
            for other in list(self.bufw.keys() | self.bufr.keys()):
                if other == b or self._overlaps(b, other):
                    w = self.bufw.get(other)
                    if w:
                        add(w[0], w[1], w[2], waw_dma_ok=True)
                    for s, v in self.bufr.get(other, {}).items():
                        add(s, v)
                    if other != b:
                        self.bufw.pop(other, None)
                        self.bufr.pop(other, None)
        waits = sorted(need.items())
        for s, v in waits:
            self.wm[eng][s] = max(self.wm[eng].get(s, 0), v)
        val = None
        if inc:
            self.cnt[semname] += inc
            val = self.cnt[semname]
        self.items[eng].append((tuple((self.sems[s], v) for s, v in waits), fn,
                                (self.sems[semname], inc) if inc else None))
        for b in writes:
            self.bufw[b] = (semname, val, eng)
            self.bufr[b] = {}
        for b in reads:
            d = self.bufr.setdefault(b, {})
            if inc and d.get(semname, 0) < val:
                d[semname] = val
        return val

    def emit(self, prologues=None):
        with self.nc.Block() as block:
            for ename in ENGINES:
                items = self.items[ename]
                pro = (prologues or {}).get(ename)
                if not items and not pro:
                    continue

                def make(items, pro):
                    def body(eng):
                        if pro:
                            pro(eng)
                        for idx, (waits, fn, inc) in enumerate(items):
                            for s, v in waits:
                                eng.wait_ge(s, v)
                            try:
                                inst = fn(eng)
                            except Exception:
                                import sys
                                print(f"EMIT FAIL eng={eng.engine} idx={idx} "
                                      f"waits={[(str(s), v) for s, v in waits]}",
                                      file=sys.stderr)
                                raise
                            if inc is not None and inst is not None:
                                inst.then_inc(inc[0], inc[1])
                    return body

                getattr(block, ename)(make(items, pro))


def build():
    nc = bass.Bass("TRN2", target_bir_lowering=False)

    x_in = nc.dram_tensor("x_in", [T, D], F32, kind="ExternalInput")
    w1t = nc.dram_tensor("w1t", [NE * D, HM], F32, kind="ExternalInput")
    w3t = nc.dram_tensor("w3t", [NE * D, HM], F32, kind="ExternalInput")
    w2t = nc.dram_tensor("w2t", [NE * HM, D], F32, kind="ExternalInput")
    wgt = nc.dram_tensor("wgt", [D, NE], F32, kind="ExternalInput")
    inpt = nc.dram_tensor("inpt", [D, 3 * D], F32, kind="ExternalInput")
    opt = nc.dram_tensor("opt", [D, D], F32, kind="ExternalInput")
    out = nc.dram_tensor("out", [T, D], F32, kind="ExternalOutput")
    qk_in = nc.dram_tensor("qk_in", [2 * D, T], BF16)
    vv_in = nc.dram_tensor("vv_in", [D, T], BF16)
    qk_out = nc.dram_tensor("qk_out", [8 * 2 * D, T], BF16, addr_space="Shared")
    vv_out = nc.dram_tensor("vv_out", [8 * D, T], BF16, addr_space="Shared")
    v_my = nc.dram_tensor("v_my", [NH * 64, SEQ], BF16)
    o_my = nc.dram_tensor("o_my", [4 * NH * 64, T], BF16)
    o_in = nc.dram_tensor("o_in", [NH * 64, SEQ], BF16)
    o_outA = nc.dram_tensor("o_outA", [8 * 2 * 64, SEQ], BF16, addr_space="Shared")
    o_outB = nc.dram_tensor("o_outB", [8 * 2 * 64, SEQ], BF16, addr_space="Shared")

    p = Prog(nc)
    ec = p.stack.enter_context
    with p.stack:
        af = ec(nc.sbuf_tensor("af", [128, F32_COLS], F32))
        ab = ec(nc.sbuf_tensor("ab", [128, BF_COLS], BF16))
        psA = ec(nc.psum_tensor("psA", [128, 1024], F32))
        psB = ec(nc.psum_tensor("psB", [128, 1024], F32))
        psC = ec(nc.psum_tensor("psC", [128, 512], F32))
        psD = ec(nc.psum_tensor("psD", [128, 512], F32))
        pst = [ec(nc.psum_tensor(f"pst{i}", [128, 1024], BF16)) for i in range(2)]
        psh = [psA[:, 0:512], psA[:, 512:1024], psB[:, 0:512], psB[:, 512:1024]]
        pshn = ["psA0", "psA1", "psB0", "psB1"]
        pscd = [psC, psD]
        pscdn = ["psC", "psD"]

        for tt in range(TT):
            p.region(f"x{tt}", "af", FX + tt * 1024, FX + (tt + 1) * 1024)
        p.region("cw", "af", FCW, FCW + 128)
        p.region("lns", "af", FLN, FLN + 32)
        p.region("gts", "af", FG, FG + 128)
        p.region("m_all0", "af", FM, FM + 64)
        p.region("m_all1", "af", FM + 64, FM + 128)
        p.region("consts", "af", FC, FC + 16)
        p.region("rscr", "af", FRS, FRS + 576)
        p.region("RfCf", "af", FRF, FRF + 1024)
        for q in range(4):
            p.region(f"zT{q}", "zTv", q * 512, (q + 1) * 512)
        for s in range(2):
            p.region(f"w13slot{s}", "ab", AW_WS[s], AW_WS[s] + 10240)
            p.region(f"w2slot{s}", "ab", AW_WS[s] + 10240, AW_WS[s] + 15360)

            p.region(f"aslot{s}", "ab", AW_AS[s], AW_AS[s] + 20544)
            p.region(f"silu{s}", "ab", AW_SI[s], AW_SI[s] + 512)
            p.region(f"PT{s}", "ab", AW_PT[s], AW_PT[s] + 1024)
            p.region(f"otok{s}", "ab", AW_OK[s], AW_OK[s] + 512)
        p.region("hp", "ab", AW_HP, AW_HP + 10240)
        p.region("ipslot", "ab", AW_IP, AW_IP + 24576)
        for i in range(4):
            p.region(f"qkev{i}", "ab", AW_QE + i * 512, AW_QE + (i + 1) * 512)
        p.region("vtscr", "ab", AW_VT, AW_VT + 1024)
        p.region("otsb", "ab", AW_OT, AW_OT + 16384)
        p.region("optsb", "ab", AW_OP, AW_OP + 8192)
        p.region("masks", "ab", B_MASK, B_MASK + 2048)
        p.region("idbf", "ab", B_ID, B_ID + 128)
        p.region("wgtsb", "ab", B_WG, B_WG + 64)
        for i in range(2):
            p.region(f"zti{i}", "ab", B_ZTI[i], B_ZTI[i] + 1024)
        p.region("mrow0", "ab", B_MROW, B_MROW + 128)
        p.region("mrow1", "ab", B_MROW + 128, B_MROW + 256)
        p.region("lnscr", "ab", B_LNS, B_LNS + 1024)
        p.region("onesb", "ab", B_ONE, B_ONE + 64)
        p.region("rcpb", "ab", B_RCP, B_RCP + 512)

        def A(base, n):
            return ab[:, base:base + n]

        def Af(base, n):
            return af[:, base:base + n]

        p.op("gpsimd", lambda e: e.iota(Af(FRF, 512), [[0, 512]], base=0,
                                        channel_multiplier=1,
                                        allow_small_or_imprecise_dtypes=True),
             writes=("RfCf",))
        p.op("gpsimd", lambda e: e.iota(Af(FCF, 512), [[1, 512]], base=0,
                                        channel_multiplier=0,
                                        allow_small_or_imprecise_dtypes=True),
             writes=("RfCf",))
        for j in range(4):
            def mk(j):
                return lambda e: e.scalar_tensor_tensor(
                    A(B_MASK + j * 512, 512), Af(FRF, 512), float(128 * j),
                    Af(FCF, 512), ALU.add, ALU.is_le)
            p.op("vector", mk(j), reads=("RfCf",), writes=("masks",))
        p.op("vector", lambda e: e.tensor_tensor(
            A(B_ID, 128), Af(FRF, 128), Af(FCF, 128), ALU.is_equal),
            reads=("RfCf",), writes=("idbf",))
        p.op("vector", lambda e: e.memset(Af(FC, 1), 1e-5), writes=("consts",))
        p.op("vector", lambda e: e.memset(Af(FC + 1, 1), -0.5), writes=("consts",))
        p.op("vector", lambda e: e.memset(Af(FC + 2, 1), 0.125), writes=("consts",))
        p.op("vector", lambda e: e.memset(Af(FC + 3, 1), -1.0), writes=("consts",))
        p.op("vector", lambda e: e.memset(ab[:, B_ONE:B_ONE + 64], 0.0),
             writes=("onesb",))
        p.op("vector", lambda e: e.memset(ab[0:1, B_ONE:B_ONE + 64], 1.0),
             writes=("onesb",))
        p.op("vector", lambda e: e.memset(ab[:, B_RCP:B_RCP + 512], 0.0),
             writes=("rcpb",))
        for dt in range(DT):
            def mk(dt):
                return lambda e: e.dma_start(out=A(B_WG + dt * 8, 8),
                                             in_=wgt[dt * 128:(dt + 1) * 128, :])
            p.op("gpsimd", mk(dt), writes=("wgtsb",), sem="dgp", inc=16)
        p.op("sync", lambda e: e.dma_start(
            out=af[:, FX:FX + 16384].rearrange("p (t d) -> p t d", d=1024),
            in_=x_in.ap().rearrange("(t p) d -> p t d", p=128)),
            writes=tuple(f"x{tt}" for tt in range(TT)), sem="dsync", inc=16)

        def zT_ap(dt, c0, n):
            return A(BZT + dt * 2048 + c0, n)

        def emit_ln(tag, gate=True):
            for tt in range(TT):
                par = tt % 2
                g = FLN + par * 16
                xs = Af(FX + tt * 1024, 1024)
                p.op("vector", lambda e, xs=xs, g=g: e.tensor_reduce(
                    Af(g, 1), xs, AX.X, ALU.add), reads=(f"x{tt}",), writes=("lns",))
                p.op("scalar", lambda e, xs=xs, g=g: e.activation(
                    A(B_LNS, 1024), xs, AF.Square, accum_out=Af(g + 1, 1)),
                    reads=(f"x{tt}",), writes=("lnscr", "lns"))
                p.op("vector", lambda e, g=g: e.tensor_scalar(
                    Af(g + 2, 1), Af(g, 1), 1.0 / D, None, ALU.mult),
                    reads=("lns",), writes=("lns",))
                p.op("vector", lambda e, g=g: e.tensor_scalar(
                    Af(g + 3, 1), Af(g + 1, 1), 1.0 / D, None, ALU.mult),
                    reads=("lns",), writes=("lns",))
                p.op("vector", lambda e, g=g: e.tensor_tensor(
                    Af(g + 4, 1), Af(g + 2, 1), Af(g + 2, 1), ALU.mult),
                    reads=("lns",), writes=("lns",))
                p.op("vector", lambda e, g=g: e.tensor_tensor(
                    Af(g + 5, 1), Af(g + 3, 1), Af(g + 4, 1), ALU.subtract),
                    reads=("lns",), writes=("lns",))
                p.op("scalar", lambda e, g=g: e.activation(
                    Af(g + 6, 1), Af(g + 5, 1), AF.Ln, bias=Af(FC, 1)),
                    reads=("lns", "consts"), writes=("lns",))
                p.op("scalar", lambda e, g=g: e.activation(
                    Af(g + 7, 1), Af(g + 6, 1), AF.Exp, scale=Af(FC + 1, 1)),
                    reads=("lns", "consts"), writes=("lns",))
                zt = A(B_ZTI[par], 1024)
                p.op("vector", lambda e, xs=xs, g=g, zt=zt: e.tensor_scalar(
                    zt, xs, Af(g + 2, 1), Af(g + 7, 1), ALU.subtract, ALU.mult),
                    reads=(f"x{tt}", "lns"), writes=(f"zti{par}",))

                def trans(e, zt=zt, par=par):
                    for dt in range(DT):
                        inst = e.transpose(pst[par][:, dt * 128:(dt + 1) * 128],
                                           zt[:, dt * 128:(dt + 1) * 128], A(B_ID, 128))
                    return inst
                p.op("tensor", trans, reads=(f"zti{par}", "idbf"), writes=(f"pst{par}",))

                def evac(e, par=par, tt=tt):
                    dst = ab[:, BZT:BZT + 16384].rearrange(
                        "p (d t) -> p d t", t=2048)[:, :, tt * 128:(tt + 1) * 128]
                    src = pst[par][:, :].rearrange("p (d t) -> p d t", t=128)
                    return e.copy(dst, src)
                p.op("scalar", evac, reads=(f"pst{par}",), writes=(f"zT{tt // 4}",))
                if gate:
                    emit_gate_tile(tt)

        def emit_gate_tile(tt):
            par = tt % 2
            b = FG + par * 64

            def mmg(e, tt=tt):
                for dt in range(DT):
                    inst = e.matmul(psC[:, 0:8], zT_ap(dt, tt * 128, 128),
                                    A(B_WG + dt * 8, 8), start=(dt == 0), stop=(dt == DT - 1))
                return inst
            p.op("tensor", mmg, reads=(f"zT{tt // 4}", "wgtsb"), writes=("psC",))
            P = psC[:, 0:8]
            p.op("vector", lambda e: e.reduce_max(Af(b, 1), P, axis=AX.X),
                 reads=("psC",), writes=("gts",))
            p.op("vector", lambda e: e.tensor_scalar(
                Af(b + 1, 1), Af(b, 1), -1.0, None, ALU.mult), reads=("gts",), writes=("gts",))
            p.op("scalar", lambda e: e.activation(
                Af(b + 16, 8), P, AF.Exp, bias=Af(b + 1, 1)),
                reads=("psC", "gts"), writes=("gts",))
            p.op("vector", lambda e: e.tensor_reduce(
                Af(b + 2, 1), Af(b + 16, 8), AX.X, ALU.add), reads=("gts",), writes=("gts",))
            p.op("vector", lambda e: e.reciprocal(Af(b + 3, 1), Af(b + 2, 1)),
                 reads=("gts",), writes=("gts",))
            p.op("vector", lambda e: e.tensor_scalar(
                Af(b + 24, 8), Af(b + 16, 8), Af(b + 3, 1), None, ALU.mult),
                reads=("gts",), writes=("gts",))
            p.op("vector", lambda e: e.reduce_max(Af(b + 4, 1), Af(b + 24, 8), axis=AX.X),
                 reads=("gts",), writes=("gts",))
            p.op("vector", lambda e: e.tensor_scalar(
                Af(b + 32, 8), Af(b + 24, 8), Af(b + 4, 1), None, ALU.is_ge),
                reads=("gts",), writes=("gts",))
            p.op("vector", lambda e: e.scalar_tensor_tensor(
                Af(b + 40, 8), Af(b + 32, 8), -1e30, Af(b + 24, 8), ALU.mult, ALU.add),
                reads=("gts",), writes=("gts",))
            p.op("vector", lambda e: e.reduce_max(Af(b + 5, 1), Af(b + 40, 8), axis=AX.X),
                 reads=("gts",), writes=("gts",))
            p.op("vector", lambda e: e.tensor_scalar(
                Af(b + 48, 8), Af(b + 40, 8), Af(b + 5, 1), None, ALU.is_equal),
                reads=("gts",), writes=("gts",))
            p.op("vector", lambda e: e.tensor_tensor(
                Af(b + 6, 1), Af(b + 4, 1), Af(b + 5, 1), ALU.add),
                reads=("gts",), writes=("gts",))
            p.op("vector", lambda e: e.tensor_scalar(
                Af(b + 7, 1), Af(b + 6, 1), 1e-20, None, ALU.add),
                reads=("gts",), writes=("gts",))
            p.op("vector", lambda e: e.reciprocal(Af(b + 8, 1), Af(b + 7, 1)),
                 reads=("gts",), writes=("gts",))
            p.op("vector", lambda e: e.tensor_scalar(
                Af(b + 9, 1), Af(b + 4, 1), Af(b + 8, 1), 0.5, ALU.mult, ALU.mult),
                reads=("gts",), writes=("gts",))
            p.op("vector", lambda e: e.tensor_scalar(
                Af(b + 10, 1), Af(b + 5, 1), Af(b + 8, 1), 0.5, ALU.mult, ALU.mult),
                reads=("gts",), writes=("gts",))
            p.op("vector", lambda e: e.tensor_scalar(
                Af(b + 56, 8), Af(b + 48, 8), Af(b + 10, 1), None, ALU.mult),
                reads=("gts",), writes=("gts",))
            p.op("vector", lambda e, tt=tt: e.scalar_tensor_tensor(
                Af(FCW + tt * 8, 8), Af(b + 32, 8), Af(b + 9, 1), Af(b + 56, 8),
                ALU.mult, ALU.add), reads=("gts",), writes=("cw",))

        def load_wchunk(e_idx, hq, slot):
            h0, h1 = HQ_TILES[hq]
            nt = h1 - h0
            hs, hw = h0 * 128, nt * 128
            base = AW_WS[slot]

            def mk13(src, boff, hs=hs, hw=hw, base=base, e_idx=e_idx):
                def f(e):
                    dst = ab[:, base + boff:base + boff + 8 * 640].rearrange(
                        "p (dt h) -> p dt h", h=640)[:, :, 0:hw]
                    return e.dma_start(
                        out=dst,
                        in_=bass.AP(src, e_idx * D * HM + hs,
                                    [[HM, 128], [128 * HM, 8], [1, hw]]))
                return f
            p.op("gpsimd", mk13(w1t, 0), writes=(f"w13slot{slot}",), sem="dgp", inc=16)
            p.op("gpsimd", mk13(w3t, 5120), writes=(f"w13slot{slot}",), sem="dgp", inc=16)

            def mk2(h0=h0, nt=nt, base=base, e_idx=e_idx):
                def f(e):
                    dst = ab[:, base + 10240:base + 10240 + nt * 1024].rearrange(
                        "p (t d) -> p t d", d=1024)
                    return e.dma_start(
                        out=dst,
                        in_=bass.AP(w2t, (e_idx * HM + h0 * 128) * D,
                                    [[D, 128], [128 * D, nt], [1, D]]))
                return f
            p.op("gpsimd", mk2(), writes=(f"w2slot{slot}",), sem="dgp", inc=16)

        def emit_ffn(tag, skip_first_load=False):
            units = [(ei, hq) for ei in EXPERTS for hq in range(len(HQ_TILES))]
            if not skip_first_load:
                load_wchunk(units[0][0], units[0][1], 0)
            for u, (ei, hq) in enumerate(units):
                slot = u % 2
                if u + 1 < len(units):
                    load_wchunk(units[u + 1][0], units[u + 1][1], (u + 1) % 2)
                h0, h1 = HQ_TILES[hq]
                nt = h1 - h0
                base = AW_WS[slot]
                for htl in range(nt):
                    for tch in range(4):
                        rot = (htl * 4 + tch) % 2

                        def mmA(e, base=base, htl=htl, tch=tch, rot=rot, woff=0):
                            for dt in range(DT):
                                inst = e.matmul(
                                    psh[rot],
                                    A(base + woff + dt * 640 + htl * 128, 128),
                                    zT_ap(dt, tch * 512, 512),
                                    start=(dt == 0), stop=(dt == DT - 1))
                            return inst
                        p.op("tensor", lambda e, f=mmA: f(e),
                             reads=(f"w13slot{slot}", f"zT{tch}"), writes=(pshn[rot],))

                        def mmB(e, base=base, htl=htl, tch=tch, rot=rot):
                            for dt in range(DT):
                                inst = e.matmul(
                                    psh[2 + rot],
                                    A(base + 5120 + dt * 640 + htl * 128, 128),
                                    zT_ap(dt, tch * 512, 512),
                                    start=(dt == 0), stop=(dt == DT - 1))
                            return inst
                        p.op("tensor", mmB, reads=(f"w13slot{slot}", f"zT{tch}"),
                             writes=(pshn[2 + rot],))
                        p.op("scalar", lambda e, rot=rot: e.activation(
                            A(AW_SI[rot], 512), psh[rot], AF.Silu),
                            reads=(pshn[rot],), writes=(f"silu{rot}",))
                        p.op("vector", lambda e, rot=rot, htl=htl, tch=tch: e.tensor_tensor(
                            A(AW_HP + htl * 2048 + tch * 512, 512),
                            A(AW_SI[rot], 512), psh[2 + rot], ALU.mult),
                            reads=(f"silu{rot}", pshn[2 + rot]), writes=("hp",))
                for tt in range(TT):
                    for dh in range(2):
                        rot = (tt * 2 + dh) % 2

                        def mmW(e, base=base, nt=nt, tt=tt, dh=dh, rot=rot):
                            for htl in range(nt):
                                inst = e.matmul(
                                    pscd[rot][:, :],
                                    A(AW_HP + htl * 2048 + tt * 128, 128),
                                    A(base + 10240 + htl * 1024 + dh * 512, 512),
                                    start=(htl == 0), stop=(htl == nt - 1))
                            return inst
                        p.op("tensor", mmW, reads=("hp", f"w2slot{slot}"),
                             writes=(pscdn[rot],))
                        p.op("vector", lambda e, tt=tt, dh=dh, rot=rot, ei=ei:
                             e.scalar_tensor_tensor(
                                 Af(FX + tt * 1024 + dh * 512, 512),
                                 pscd[rot][:, :], Af(FCW + tt * 8 + ei, 1),
                                 Af(FX + tt * 1024 + dh * 512, 512), ALU.mult, ALU.add),
                             reads=(pscdn[rot], "cw", f"x{tt}"), writes=(f"x{tt}",))

        def emit_qkv(regs):
            nfc = 12
            for half in range(2):
                def mkip(half=half):
                    def f(e):
                        dst = ab[:, AW_IP:AW_IP + 24576].rearrange(
                            "p (dt f) -> p dt f", f=3072)[:, :, half * 1536:(half + 1) * 1536]
                        return e.dma_start(
                            out=dst,
                            in_=bass.AP(inpt, half * 1536,
                                        [[3 * D, 128], [128 * 3 * D, 8], [1, 1536]]))
                    return f
                p.op("gpsimd", mkip(), writes=("ipslot",), sem="dgp", inc=16)
            for fc in range(nfc):
                for ft in range(2):
                    for tch in range(4):
                        rot = (ft * 4 + tch) % 4
                        f0 = fc * 256 + ft * 128

                        def mmQ(e, fc=fc, ft=ft, tch=tch, rot=rot):
                            for dt in range(DT):
                                inst = e.matmul(
                                    psh[rot],
                                    A(AW_IP + dt * 3072 + fc * 256 + ft * 128, 128),
                                    zT_ap(dt, tch * 512, 512),
                                    start=(dt == 0), stop=(dt == DT - 1))
                            return inst
                        p.op("tensor", mmQ, reads=("ipslot", f"zT{tch}"),
                             writes=(pshn[rot],))
                        ev = rot
                        if f0 < D:
                            p.op("scalar", lambda e, rot=rot, ev=ev: e.activation(
                                A(AW_QE + ev * 512, 512), psh[rot], AF.Copy,
                                scale=Af(FC + 2, 1)),
                                reads=(pshn[rot], "consts"), writes=(f"qkev{ev}",))
                        else:
                            p.op("scalar", lambda e, rot=rot, ev=ev: e.copy(
                                A(AW_QE + ev * 512, 512), psh[rot]),
                                reads=(pshn[rot],), writes=(f"qkev{ev}",))
                        if f0 < 2 * D:
                            p.op("sync", lambda e, f0=f0, tch=tch, ev=ev: e.dma_start(
                                out=qk_in[f0:f0 + 128, tch * 512:(tch + 1) * 512],
                                in_=A(AW_QE + ev * 512, 512)),
                                reads=(f"qkev{ev}",), writes=("qk_in",),
                                sem="dsync", inc=16)
                        else:
                            p.op("sync", lambda e, f0=f0, tch=tch, ev=ev: e.dma_start(
                                out=vv_in[f0 - 2 * D:f0 - 2 * D + 128,
                                          tch * 512:(tch + 1) * 512],
                                in_=A(AW_QE + ev * 512, 512)),
                                reads=(f"qkev{ev}",), writes=("vv_in",),
                                sem="dsync", inc=16)
                if fc == 7:
                    def cc1a(e):
                        return e.collective_compute(
                            "AllGather", ALU.bypass,
                            replica_groups=[list(range(8))],
                            ins=[qk_in.ap().opt()], outs=[qk_out.ap().opt()])
                    p.op("gpsimd", cc1a, reads=("qk_in",), writes=("qk_out",),
                         sem="cc1a", inc=1)

            def cc1b(e):
                return e.collective_compute(
                    "AllGather", ALU.bypass, replica_groups=[list(range(8))],
                    ins=[vv_in.ap().opt()], outs=[vv_out.ap().opt()])
            p.op("gpsimd", cc1b, reads=("vv_in",), writes=("vv_out",), sem="cc1b", inc=1)

            def vstage(e):
                return e.dma_start(
                    out=v_my.ap().rearrange("r (i t) -> r i t", i=4),
                    in_=bass.AP(vv_out, regs["rvv"],
                                [[T, 256], [1024 * T, 4], [1, T]]))
            p.op("sync", vstage, reads=("vv_out",), writes=("v_my",),
                 sem="dsync", inc=16)

        def kcount(tqt):
            return tqt // 4 + 1

        def prep_ops(h, regs):
            sl = h % 2
            par = h % 2
            qb, kb, vb = AW_AS[sl], AW_AS[sl] + 8192, AW_AS[sl] + 16384
            aslot = f"aslot{sl}"
            fmh = FM + par * 64
            mrb = B_MROW + par * 128
            mname = f"m_all{par}"
            rname = f"mrow{par}"
            ops_io = []
            ops = ops_io
            ops.append(lambda kb=kb, aslot=aslot: p.op(
                "vector", lambda e: e.memset(ab[64:65, kb:kb + 8192], 1.0),
                writes=(aslot,)))
            ops.append(lambda vb=vb, aslot=aslot: p.op(
                "vector", lambda e: e.memset(
                    ab[:, vb:vb + 4160].rearrange(
                        "p (t c) -> p t c", c=65)[:, :, 64:65], 1.0),
                writes=(aslot,)))

            def mkq(e, h=h, qb=qb):
                e.reg_add(regs["scr"], regs["rqk"], h * 64 * T)
                return e.dma_start(
                    out=ab[0:64, qb:qb + 8192].rearrange("p (i t) -> p i t", i=4),
                    in_=bass.AP(qk_out, regs["scr"],
                                [[T, 64], [2048 * T, 4], [1, T]]))
            ops.append(lambda f=mkq, aslot=aslot: p.op(
                "sync", f, reads=("qk_out",), writes=(aslot,), sem="dsync", inc=16))

            def mkk(e, h=h, kb=kb):
                e.reg_add(regs["scr"], regs["rqk"], (1024 + h * 64) * T)
                return e.dma_start(
                    out=ab[0:64, kb:kb + 8192].rearrange("p (i t) -> p i t", i=4),
                    in_=bass.AP(qk_out, regs["scr"],
                                [[T, 64], [2048 * T, 4], [1, T]]))
            ops.append(lambda f=mkk, aslot=aslot: p.op(
                "sync", f, reads=("qk_out",), writes=(aslot,), sem="dsync", inc=16))

            for i in range(4):
                for half in range(2):
                    c0 = i * 2048 + half * 1024

                    def mkv(e, h=h, c0=c0):
                        return e.dma_start(
                            out=ab[0:64, AW_VT:AW_VT + 1024],
                            in_=v_my[h * 64:(h + 1) * 64, c0:c0 + 1024])
                    ops.append(lambda f=mkv: p.op(
                        "sync", f, reads=("v_my",), writes=("vtscr",),
                        sem="dsync", inc=16))

                    def vtr(e, half=half):
                        for j in range(8):
                            inst = e.transpose(
                                pst[half][:, j * 128:(j + 1) * 128],
                                ab[0:128, AW_VT + j * 128:AW_VT + (j + 1) * 128],
                                A(B_ID, 128))
                        return inst
                    ops.append(lambda f=vtr, half=half: p.op(
                        "tensor", f, reads=("vtscr", "idbf"), writes=(f"pst{half}",)))

                    def vev(e, half=half, c0=c0, vb=vb):
                        dst = ab[:, vb:vb + 4160].rearrange(
                            "p (t c) -> p t c", c=65)[:, c0 // 128:c0 // 128 + 8, 0:64]
                        src = pst[half][:, 0:1024].rearrange(
                            "p (t c) -> p t c", c=128)[:, :, 0:64]
                        return e.copy(dst, src)
                    ops.append(lambda f=vev, half=half, aslot=aslot: p.op(
                        "scalar", f, reads=(f"pst{half}",), writes=(aslot,)))

            ops_p1 = []
            ops = ops_p1
            ops.append(lambda qb=qb, aslot=aslot: p.op(
                "vector", lambda e: e.memset(ab[64:65, qb:qb + 8192], 0.0),
                writes=(aslot,)))
            for tqt in range(NTQ):
                nk = kcount(tqt)
                nsamp = nk * 32
                X = psA if tqt % 2 == 0 else psB
                Xn = ("psA0", "psA1") if tqt % 2 == 0 else ("psB0", "psB1")

                def mmS(e, tqt=tqt, nsamp=nsamp, X=X, qb=qb, kb=kb):
                    rhs = ab[0:65, kb:kb + nsamp * 16].rearrange(
                        "p (n s) -> p n s", s=16)[:, :, 0:1]
                    return e.matmul(X[:, 0:nsamp],
                                    ab[0:65, qb + tqt * 128:qb + (tqt + 1) * 128],
                                    rhs, start=True, stop=True)
                ops.append(lambda f=mmS, Xn=Xn, aslot=aslot: p.op(
                    "tensor", f, reads=(aslot,), writes=Xn))

                def red(e, tqt=tqt, nsamp=nsamp, X=X, fmh=fmh):
                    return e.reduce_max(Af(fmh + tqt, 1), X[:, 0:nsamp], axis=AX.X)
                ops.append(lambda f=red, Xn=Xn, mname=mname: p.op(
                    "vector", f, reads=Xn, writes=(mname,)))

            ops.append(lambda fmh=fmh, mrb=mrb, mname=mname, rname=rname: p.op(
                "vector", lambda e: e.tensor_scalar(
                    A(mrb, 64), Af(fmh, 64), -1.0, None, ALU.mult),
                reads=(mname,), writes=(rname,)))
            ops.append(lambda mrb=mrb, par=par, rname=rname: p.op(
                "tensor", lambda e: e.transpose(
                    pst[par][0:64, 0:128], A(mrb, 64), A(B_ID, 128)),
                reads=(rname, "idbf"), writes=(f"pst{par}",)))
            ops.append(lambda mrb=mrb, par=par, rname=rname: p.op(
                "scalar", lambda e: e.copy(ab[0:64, mrb:mrb + 128],
                                           pst[par][0:64, 0:128]),
                reads=(f"pst{par}",), writes=(rname,)))
            ops.append(lambda mrb=mrb, qb=qb, rname=rname, aslot=aslot: p.op(
                "sync", lambda e: e.dma_start(
                    out=ab[64:65, qb:qb + 8192],
                    in_=ab[0:64, mrb:mrb + 128]),
                reads=(rname,), writes=(aslot,), sem="dsync", inc=16))
            return ops_io, ops_p1

        def emit_phase2(h, regs, inject=()):
            sl = h % 2
            qb, kb, vb = AW_AS[sl], AW_AS[sl] + 8192, AW_AS[sl] + 16384
            aslot = f"aslot{sl}"
            inject = list(inject)
            inj_i = [0]
            pend_av = [None]
            pend_norm = [None]

            def emit_norm(tqc):
                av = tqc % 2
                avb, avn = pscd[av], pscdn[av]
                ok = tqc % 2
                p.op("scalar", lambda e, avb=avb: e.activation(
                    af[0:1, FRS:FRS + 512], avb[64:65, :], AF.Ln),
                    reads=(avn,), writes=("rscr",))

                def expf(e):
                    with nc.allow_low_precision(reason="softmax denom 1/l in bf16"):
                        return e.activation(ab[0:1, B_RCP:B_RCP + 512],
                                            af[0:1, FRS:FRS + 512], AF.Exp,
                                            scale=af[0:1, FC + 3:FC + 4])
                p.op("scalar", expf, reads=("rscr", "consts"), writes=("rcpb",))
                p.op("scalar", lambda e, avb=avb, ok=ok: e.copy(
                    ab[0:64, AW_OK[ok]:AW_OK[ok] + 512], avb[0:64, :]),
                    reads=(avn,), writes=(f"otok{ok}",))

                def bcast(e, avb=avb):
                    return e.matmul(avb[0:64, :], ab[:, B_ONE:B_ONE + 64],
                                    ab[:, B_RCP:B_RCP + 512], start=True, stop=True)
                p.op("tensor", bcast, reads=("rcpb", "onesb"), writes=(avn,))
                p.op("vector", lambda e, avb=avb, ok=ok: e.tensor_tensor(
                    ab[0:64, AW_OK[ok]:AW_OK[ok] + 512],
                    ab[0:64, AW_OK[ok]:AW_OK[ok] + 512],
                    avb[0:64, :], ALU.mult),
                    reads=(avn, f"otok{ok}"), writes=(f"otok{ok}",))
                p.op("sync", lambda e, h=h, tqc=tqc, ok=ok: e.dma_start(
                    out=o_in[h * 64:(h + 1) * 64, tqc * 512:(tqc + 1) * 512],
                    in_=ab[0:64, AW_OK[ok]:AW_OK[ok] + 512]),
                    reads=(f"otok{ok}",), writes=("o_in",), sem="dsync", inc=16)

            for tqc in range(16):
                av = tqc % 2
                avb, avn = pscd[av], pscdn[av]
                ntk = (tqc + 1) * 4
                for tp in range(ntk // 2):
                    X = psA if tp % 2 == 0 else psB
                    Xn = ("psA0", "psA1") if tp % 2 == 0 else ("psB0", "psB1")
                    pt = tp % 2

                    def mmT2(e, tp=tp, tqc=tqc, X=X, qb=qb, kb=kb):
                        for half in range(2):
                            tkt = 2 * tp + half
                            inst = e.matmul(
                                X[:, half * 512:(half + 1) * 512],
                                ab[0:65, kb + tkt * 128:kb + (tkt + 1) * 128],
                                ab[0:65, qb + tqc * 512:qb + (tqc + 1) * 512],
                                start=True, stop=True)
                        return inst
                    p.op("tensor", mmT2, reads=(aslot,), writes=Xn)
                    p.op("scalar", lambda e, X=X, pt=pt: e.activation(
                        A(AW_PT[pt], 1024), X[:, :], AF.Exp),
                        reads=Xn, writes=(f"PT{pt}",))
                    j0 = 2 * tp - 4 * tqc
                    if 0 <= j0 <= 3:
                        p.op("vector", lambda e, pt=pt, j0=j0: e.tensor_tensor(
                            A(AW_PT[pt], 1024), A(AW_PT[pt], 1024),
                            A(B_MASK + j0 * 512, 1024), ALU.mult),
                            reads=(f"PT{pt}", "masks"), writes=(f"PT{pt}",))
                    if pend_av[0] is not None:
                        fn, rd, wr = pend_av[0]
                        p.op("tensor", fn, reads=rd, writes=wr)
                        pend_av[0] = None
                    if pend_norm[0] is not None:
                        emit_norm(pend_norm[0])
                        pend_norm[0] = None
                    if inj_i[0] < len(inject):
                        inject[inj_i[0]]()
                        inj_i[0] += 1

                    def mmAV(e, tp=tp, pt=pt, avb=avb, vb=vb):
                        for half in range(2):
                            tkt = 2 * tp + half
                            inst = e.matmul(
                                avb[0:65, :],
                                A(vb + tkt * 65, 65),
                                A(AW_PT[pt] + half * 512, 512),
                                start=(tp == 0 and half == 0), stop=False,
                                skip_group_check=True)
                        return inst
                    pend_av[0] = (mmAV, (f"PT{pt}", aslot), (avn,))
                fn, rd, wr = pend_av[0]
                p.op("tensor", fn, reads=rd, writes=wr)
                pend_av[0] = None
                if pend_norm[0] is not None:
                    emit_norm(pend_norm[0])
                pend_norm[0] = tqc
            emit_norm(pend_norm[0])
            while inj_i[0] < len(inject):
                inject[inj_i[0]]()
                inj_i[0] += 1

        def emit_attn(regs):
            p0io, p0p1 = prep_ops(HEADS[0], regs)
            for cl in p0io + p0p1:
                cl()
            for idx, h in enumerate(HEADS):
                if idx + 1 < len(HEADS):
                    nio, np1 = prep_ops(HEADS[idx + 1], regs)
                else:
                    nio, np1 = [], []
                emit_phase2(h, regs, inject=nio)
                for cl in np1:
                    cl()
                if h == 1:
                    def cc3a(e):
                        return e.collective_compute(
                            "AllGather", ALU.bypass,
                            replica_groups=[list(range(8))],
                            ins=[o_in[0:128, :].opt()], outs=[o_outA.ap().opt()])
                    p.op("gpsimd", cc3a, reads=("o_in",), writes=("o_outA",),
                         sem="cc3a", inc=1)
            if DO_FFN2:
                load_wchunk(EXPERTS[0], 0, 0)

            def cc3b(e):
                return e.collective_compute(
                    "AllGather", ALU.bypass, replica_groups=[list(range(8))],
                    ins=[o_in[128:256, :].opt()], outs=[o_outB.ap().opt()])
            p.op("gpsimd", cc3b, reads=("o_in",), writes=("o_outB",), sem="cc3b", inc=1)
            for et in range(DT):
                def mko(et=et):
                    return lambda e: e.dma_start(
                        out=A(AW_OP + et * 1024, 1024),
                        in_=opt[et * 128:(et + 1) * 128, :])
                p.op("gpsimd", mko(), writes=("optsb",), sem="dgp", inc=16)
            def ostageA(e):
                dst = o_my.ap().rearrange("(j x) t -> j x t", x=256)[:, 0:128, :]
                return e.dma_start(
                    out=dst,
                    in_=bass.AP(o_outA, regs["roth"],
                                [[128 * SEQ, 4], [SEQ, 128], [1, T]]))
            p.op("sync", ostageA, reads=("o_outA",), writes=("o_my",),
                 sem="dsync", inc=16)

            def ostageB(e):
                dst = o_my.ap().rearrange("(j x) t -> j x t", x=256)[:, 128:256, :]
                return e.dma_start(
                    out=dst,
                    in_=bass.AP(o_outB, regs["roth"],
                                [[128 * SEQ, 4], [SEQ, 128], [1, T]]))
            p.op("sync", ostageB, reads=("o_outB",), writes=("o_my",),
                 sem="dsync", inc=16)
            for et in range(DT):
                def mkt(et=et):
                    return lambda e: e.dma_start(
                        out=A(AW_OT + et * 2048, 2048),
                        in_=o_my[et * 128:(et + 1) * 128, :])
                p.op("sync", mkt(), reads=("o_my",), writes=("otsb",),
                     sem="dsync", inc=16)
            for tt in range(TT):
                for fh in range(2):
                    rot = (tt * 2 + fh) % 4

                    def mmo(e, tt=tt, fh=fh, rot=rot):
                        for et in range(DT):
                            inst = e.matmul(
                                psh[rot],
                                A(AW_OT + et * 2048 + tt * 128, 128),
                                A(AW_OP + et * 1024 + fh * 512, 512),
                                start=(et == 0), stop=(et == DT - 1))
                        return inst
                    p.op("tensor", mmo, reads=("otsb", "optsb"), writes=(pshn[rot],))
                    p.op("vector", lambda e, tt=tt, fh=fh, rot=rot: e.tensor_tensor(
                        Af(FX + tt * 1024 + fh * 512, 512),
                        Af(FX + tt * 1024 + fh * 512, 512),
                        psh[rot], ALU.add),
                        reads=(pshn[rot], f"x{tt}"), writes=(f"x{tt}",))

        if DO_FFN1:
            emit_ln("ln1")
            emit_ffn("ffn1")
        if DO_ATTN:
            regs = {}
            emit_ln("ln2", gate=False)
            emit_qkv(regs)

            def sync_pro(e):
                regs["rank"] = e.alloc_register(name="r_rank")
                regs["bg"] = e.alloc_register(name="r_bg")
                regs["rm"] = e.alloc_register(name="r_rm")
                regs["rqk"] = e.alloc_register(name="r_rqk")
                regs["rvv"] = e.alloc_register(name="r_rvv")
                regs["roth"] = e.alloc_register(name="r_roth")
                regs["scr"] = e.alloc_register(name="r_scr")
                e.cc_rank_ld(regs["rank"], replica_groups=[list(range(8))])
                e.reg_mov(regs["scr"], 4)
                e.reg_div(regs["bg"], regs["rank"], regs["scr"])
                e.reg_mod(regs["rm"], regs["rank"], regs["scr"])
                e.reg_mov(regs["scr"], 4 * 2048 * T)
                e.reg_mul(regs["rqk"], regs["bg"], regs["scr"])
                e.reg_mov(regs["scr"], 4 * 64 * T)
                e.reg_mul(regs["scr"], regs["rm"], regs["scr"])
                e.reg_add(regs["rqk"], regs["rqk"], regs["scr"])
                e.reg_mov(regs["scr"], 4 * 1024 * T)
                e.reg_mul(regs["rvv"], regs["bg"], regs["scr"])
                e.reg_mov(regs["scr"], 4 * 64 * T)
                e.reg_mul(regs["scr"], regs["rm"], regs["scr"])
                e.reg_add(regs["rvv"], regs["rvv"], regs["scr"])
                e.reg_mov(regs["scr"], 4 * 128 * SEQ)
                e.reg_mul(regs["roth"], regs["bg"], regs["scr"])
                e.reg_mov(regs["scr"], T)
                e.reg_mul(regs["scr"], regs["rm"], regs["scr"])
                e.reg_add(regs["roth"], regs["roth"], regs["scr"])
            emit_attn(regs)
        else:
            sync_pro = None
        if DO_FFN2:
            emit_ln("ln3")
            emit_ffn("ffn2", skip_first_load=DO_ATTN)
        p.op("sync", lambda e: e.dma_start(
            out=out.ap().rearrange("(t p) d -> p t d", p=128),
            in_=af[:, FX:FX + 16384].rearrange("p (t d) -> p t d", d=1024)),
            reads=tuple(f"x{tt}" for tt in range(TT)), sem="dsync", inc=16)
        final = p.cnt["dsync"]
        p.op("sync", lambda e, final=final: e.wait_ge(p.sems["dsync"], final), inc=0)

        p.emit(prologues={"sync": sync_pro} if DO_ATTN else None)
    return nc


_NC_CACHE = None


def kernel(L0_attn_query, L0_attn_key, L0_attn_value, L0_attn_in_proj_weight,
           L0_attn_out_proj_weight, L0_ffn_Wg, L0_ffn_W1, L0_ffn_W2, L0_ffn_W3):
    global _NC_CACHE
    bfloat16 = ml_dtypes.bfloat16
    x = np.ascontiguousarray(np.asarray(L0_attn_query, np.float32))
    assert np.asarray(L0_attn_key).shape == (4, 2048, 1024)
    assert np.asarray(L0_attn_value).shape == (4, 2048, 1024)
    W1 = np.asarray(L0_ffn_W1, np.float32)
    W2 = np.asarray(L0_ffn_W2, np.float32)
    W3 = np.asarray(L0_ffn_W3, np.float32)
    Wg = np.asarray(L0_ffn_Wg, np.float32)
    ip = np.asarray(L0_attn_in_proj_weight, np.float32)
    op = np.asarray(L0_attn_out_proj_weight, np.float32)

    w1t = np.ascontiguousarray(W1.transpose(0, 2, 1)).reshape(NE * D, HM)
    w3t = np.ascontiguousarray(W3.transpose(0, 2, 1)).reshape(NE * D, HM)
    w2t = np.ascontiguousarray(W2.transpose(0, 2, 1)).reshape(NE * HM, D)
    wgt = np.ascontiguousarray(Wg.T)
    inpt = np.ascontiguousarray(ip.T)
    opt_ = np.ascontiguousarray(op.T)

    if _NC_CACHE is None:
        _NC_CACHE = build()
    nc = _NC_CACHE
    in_maps = []
    for c in range(8):
        b, ch = c // 4, c % 4
        in_maps.append(dict(
            x_in=np.ascontiguousarray(x[b, ch * T:(ch + 1) * T]),
            w1t=w1t, w3t=w3t, w2t=w2t, wgt=wgt, inpt=inpt, opt=opt_,
        ))
    res = run_bass_kernel_spmd(nc, in_maps, core_ids=list(range(8)))
    outp = np.empty((2, SEQ, D), np.float32)
    for c in range(8):
        b, ch = c // 4, c % 4
        outp[b, ch * T:(ch + 1) * T] = res.results[c]["out"]
    return outp
